# version 11
# baseline (speedup 1.0000x reference)
import numpy as np
from contextlib import ExitStack
import concourse.bass as bass
import concourse.mybir as mybir
from concourse.bass_utils import run_bass_kernel_spmd

F32 = mybir.dt.float32
BF16 = mybir.dt.bfloat16
AF = mybir.ActivationFunctionType
ALU = mybir.AluOpType
AX = mybir.AxisListType

B, S, D, H, E, HID = 8, 2048, 1024, 16, 64, 4096
KD = D // 128
KH = HID // 128
NPAIR = H // 2
TC = S // 128
QS = float(8.0 ** -0.5)
LN_EPS = 1e-5
THALF = S // 2

AR = {
    "xbuf":    (0,      65536),
    "xsq":     (65536,  16384),
    "rowbuf1": (81920,  16384),
    "sqbf1":   (98304,  8192),
    "wvld":    (106496, 16384),
    "wkld":    (122880, 16384),
    "wqld":    (139264, 16384),
    "wo_sb":   (155648, 16384),
    "q_sb":    (0,      32768),
    "k_sb":    (32768,  32768),
    "vt_sb":   (65536,  33280),
    "bvrep":   (98816,  4096),
    "q2buf":   (102912, 4096),
    "qm2":     (139264, 256),
    "crow":    (139520, 128),
    "denom":   (106496, 8192),
    "selpair": (98816,  4096),
    "pbuf":    (122880, 8192),
    "dtmp":    (131072, 8192),
    "attn_n":  (0,      32768),
    "x2buf":   (32768,  65536),
    "rtile":   (114688, 8192),
    "xrb":     (122880, 16384),
    "xsq2":    (0,      16384),
    "rowbuf2": (16384,  16384),
    "sqbf2":   (98304,  8192),
    "hbufA":   (0,      65536),
    "hbufB":   (65536,  65536),
    "wb0":     (131072, 8192),
    "wb1":     (139264, 8192),
    "x2rb":    (147456, 8192),
    "ostg":    (155648, 8192),
}
ARENA_SIZE = 172032


def build_nc():
    nc = bass.Bass("TRN2", target_bir_lowering=False)

    xst = nc.dram_tensor("xst", [128, KD * S], F32, kind="ExternalInput").ap()
    wq_d = nc.dram_tensor("wq", [128, 8192], F32, kind="ExternalInput").ap()
    wk_d = nc.dram_tensor("wk", [128, 8192], F32, kind="ExternalInput").ap()
    wv_d = nc.dram_tensor("wv", [128, 8192], F32, kind="ExternalInput").ap()
    wo_d = nc.dram_tensor("wo", [128, 8192], F32, kind="ExternalInput").ap()
    bq_d = nc.dram_tensor("bq", [128, 8], F32, kind="ExternalInput").ap()
    bk_d = nc.dram_tensor("bk", [128, 8], F32, kind="ExternalInput").ap()
    bv_d = nc.dram_tensor("bvr", [1, 1024], F32, kind="ExternalInput").ap()
    bo_d = nc.dram_tensor("bo", [128, 8], F32, kind="ExternalInput").ap()
    w1_d = nc.dram_tensor("w1", [KH, 128, KD * 128], F32, kind="ExternalInput").ap()
    w2_d = nc.dram_tensor("w2", [KH, 128, KD * 128], F32, kind="ExternalInput").ap()
    h1_d = nc.dram_tensor("h1", [KH, 128, KH * 128], F32, kind="ExternalInput").ap()
    h2_d = nc.dram_tensor("h2", [KH, 128, KH * 128], F32, kind="ExternalInput").ap()
    w3_d = nc.dram_tensor("w3", [KD, 128, KH * 128], F32, kind="ExternalInput").ap()
    s1_d = nc.dram_tensor("s1", [128, KH], F32, kind="ExternalInput").ap()
    s2_d = nc.dram_tensor("s2", [128, KH], F32, kind="ExternalInput").ap()
    sh1_d = nc.dram_tensor("sh1", [128, KH], F32, kind="ExternalInput").ap()
    sh2_d = nc.dram_tensor("sh2", [128, KH], F32, kind="ExternalInput").ap()
    s3_d = nc.dram_tensor("s3", [128, KD], F32, kind="ExternalInput").ap()
    out_fm = nc.dram_tensor("out_fm", [128, KD * S], F32, kind="ExternalOutput").ap()
    x2hbm = nc.dram_tensor("x2hbm", [128, KD * S], F32, kind="Internal").ap()

    ones_c = nc.inline_tensor(np.ones((1, 128), np.float32), "ones_c").ap()
    onescol_c = nc.inline_tensor(np.ones((128, 1), np.float32), "onescol_c").ap()
    sel = np.zeros((128, NPAIR * 16), np.float32)
    for j in range(NPAIR):
        sel[0:64, j * 16 + 2 * j] = 1.0
        sel[64:128, j * 16 + 2 * j + 1] = 1.0
    sel16_c = nc.inline_tensor(sel, "sel16_c").ap()
    sp = np.zeros((16, NPAIR * 128), np.float32)
    for j in range(NPAIR):
        sp[2 * j, j * 128: j * 128 + 64] = 1.0
        sp[2 * j + 1, j * 128 + 64: j * 128 + 128] = 1.0
    selpair_c = nc.inline_tensor(sp, "selpair_c").ap()
    id16_c = nc.inline_tensor(np.eye(16, dtype=np.float32), "id16_c").ap()

    ones_sb = nc.alloc_sbuf_tensor("ones_sb", [1, 128], F32)[:]
    onescol_sb = nc.alloc_sbuf_tensor("onescol_sb", [128, 1], F32)[:]
    onescol_bf = nc.alloc_sbuf_tensor("onescol_bf", [128, 1], BF16)[:]
    id16_sb = nc.alloc_sbuf_tensor("id16_sb", [16, 16], F32)[:]
    sel16_sb = nc.alloc_sbuf_tensor("sel16_sb", [128, NPAIR * 16], BF16)[:]
    bqs = nc.alloc_sbuf_tensor("bqs", [128, 8], F32)[:]
    bks = nc.alloc_sbuf_tensor("bks", [128, 8], F32)[:]
    bo_sb = nc.alloc_sbuf_tensor("bo_sb", [128, 8], F32)[:]
    bvr_sb = nc.alloc_sbuf_tensor("bvr_sb", [1, 1024], F32)[:]
    s1_sb = nc.alloc_sbuf_tensor("s1_sb", [128, KH], F32)[:]
    s2_sb = nc.alloc_sbuf_tensor("s2_sb", [128, KH], F32)[:]
    sh1_sb = nc.alloc_sbuf_tensor("sh1_sb", [128, KH], F32)[:]
    sh2_sb = nc.alloc_sbuf_tensor("sh2_sb", [128, KH], F32)[:]
    s3_sb = nc.alloc_sbuf_tensor("s3_sb", [128, KD], F32)[:]
    cneg = nc.alloc_sbuf_tensor("cneg", [128, NPAIR], F32)[:]
    act_bf = nc.alloc_sbuf_tensor("act_bf", [128, KD * S], BF16)[:]

    arena_base = (nc.sbuf_base + 255) & ~255
    assert arena_base + ARENA_SIZE <= nc.sbuf_top, (
        f"arena overflow: base={arena_base} size={ARENA_SIZE} top={nc.sbuf_top}")

    def arena(name, shape, dtype):
        off, sz = AR[name]
        esz = 2 if dtype == BF16 else 4
        nbytes = int(np.prod(shape[1:])) * esz
        assert nbytes <= sz, (name, nbytes, sz)
        return nc.alloc_sbuf_tensor_at(name, shape, dtype,
                                       offset=arena_base + off)[:]

    xbuf = arena("xbuf", [128, KD * S], F32)
    xsq = arena("xsq", [128, 2 * S], F32)
    sqbf1 = arena("sqbf1", [128, 2 * S], BF16)
    rowbuf1 = arena("rowbuf1", [1, 2 * S], F32)
    wv_sb = arena("wvld", [128, 8192], BF16)
    wk_sb = arena("wkld", [128, 8192], BF16)
    wq_sb = arena("wqld", [128, 8192], BF16)
    q_sb = arena("q_sb", [128, NPAIR * S], BF16)
    k_sb = arena("k_sb", [128, NPAIR * S], BF16)
    vt_sb = arena("vt_sb", [128, TC * 16 * 65], BF16)
    bvrep_sb = arena("bvrep", [128, 1024], F32)
    q2buf = arena("q2buf", [128, S], BF16)
    qm2 = arena("qm2", [16, 2], F32)
    crow = arena("crow", [1, 32], F32)
    denom_sb = arena("denom", [16, S], F32)
    selpair_sb = arena("selpair", [16, NPAIR * 128], F32)
    pbuf = arena("pbuf", [128, 2 * S], BF16)
    dtmp = arena("dtmp", [65, 2 * 1024], F32)
    wo_sb = arena("wo_sb", [128, 8192], BF16)
    attn_n = arena("attn_n", [128, NPAIR * S], BF16)
    x2buf = arena("x2buf", [128, KD * S], F32)
    rtile = arena("rtile", [16, S], F32)
    xrb = arena("xrb", [128, 2 * S], F32)
    xsq2 = arena("xsq2", [128, 2 * S], F32)
    sqbf2 = arena("sqbf2", [128, 2 * S], BF16)
    rowbuf2 = arena("rowbuf2", [1, 2 * S], F32)
    hbufA = arena("hbufA", [128, KH * THALF], BF16)
    hbufB = arena("hbufB", [128, KH * THALF], BF16)
    wb0 = arena("wb0", [128, KH * 128], BF16)
    wb1 = arena("wb1", [128, KH * 128], BF16)
    x2rb = arena("x2rb", [128, 2 * 1024], F32)
    ostg = arena("ostg", [128, 2 * 1024], F32)

    n = {"v": 0, "sc": 0, "pe": 0, "exp": 0, "att": 0, "scx": 0, "dcp": 0,
         "wc": 0, "mm": 0, "evs": 0, "evv": 0}

    es = ExitStack()
    with es:
        s_x = es.enter_context(nc.semaphore("s_x"))
        s_const = es.enter_context(nc.semaphore("s_const"))
        s_wqkv = es.enter_context(nc.semaphore("s_wqkv"))
        s_small = es.enter_context(nc.semaphore("s_small"))
        s_v = es.enter_context(nc.semaphore("s_v"))
        s_sc = es.enter_context(nc.semaphore("s_sc"))
        s_pe = es.enter_context(nc.semaphore("s_pe"))
        s_exp = es.enter_context(nc.semaphore("s_exp"))
        s_att = es.enter_context(nc.semaphore("s_att"))
        s_scx = es.enter_context(nc.semaphore("s_scx"))
        s_dcp = es.enter_context(nc.semaphore("s_dcp"))
        s_wl0 = es.enter_context(nc.semaphore("s_wl0"))
        s_wl1 = es.enter_context(nc.semaphore("s_wl1"))
        s_wc = es.enter_context(nc.semaphore("s_wc"))
        s_mm = es.enter_context(nc.semaphore("s_mm"))
        s_evs = es.enter_context(nc.semaphore("s_evs"))
        s_evv = es.enter_context(nc.semaphore("s_evv"))
        s_xr0 = es.enter_context(nc.semaphore("s_xr0"))
        s_xr1 = es.enter_context(nc.semaphore("s_xr1"))
        s_o0 = es.enter_context(nc.semaphore("s_o0"))
        s_o1 = es.enter_context(nc.semaphore("s_o1"))
        s_x2st = es.enter_context(nc.semaphore("s_x2st"))
        s_dd0 = es.enter_context(nc.semaphore("s_dd0"))
        s_dd1 = es.enter_context(nc.semaphore("s_dd1"))

        swl = [s_wl0, s_wl1]
        sxr = [s_xr0, s_xr1]
        sout = [s_o0, s_o1]
        sdd = [s_dd0, s_dd1]

        def vop(v, meth, *a, **kw):
            v.wait_ge(s_v, n["v"])
            getattr(v, meth)(*a, **kw).then_inc(s_v, 1)
            n["v"] += 1
            return n["v"]

        def scop(sc, meth, *a, **kw):
            sc.wait_ge(s_sc, n["sc"])
            getattr(sc, meth)(*a, **kw).then_inc(s_sc, 1)
            n["sc"] += 1
            return n["sc"]

        def emit_layernorm(src, xsqb, sqbf, rowb, tag):
            rowm = rowb[0:1, 0:S]
            rowr = rowb[0:1, S:2 * S]
            rowt = xsqb[0:1, 0:S]
            rowu = xsqb[0:1, S:2 * S]
            with (
                nc.psum_tensor("ps_sx" + tag, [1, S], F32) as ps_sx,
                nc.psum_tensor("ps_sq" + tag, [1, S], F32) as ps_sq,
            ):
                with nc.Block() as blk:
                    pe0ln = n["pe"]

                    @blk.scalar
                    def _(sc):
                        for k in range(KD):
                            if k >= 2:
                                sc.wait_ge(s_pe, pe0ln + k - 1)
                            scop(sc, "activation", sqbf[:, (k % 2) * S:(k % 2) * S + S],
                                 src[:, k * S:(k + 1) * S], AF.Square)

                    sc0 = n["sc"] - KD

                    @blk.tensor
                    def _(t):
                        t.wait_ge(s_const, 48)
                        t.wait_ge(s_small, 16 * 11)
                        for k in range(KD):
                            for nn in range(4):
                                t.matmul(ps_sx[:, nn * 512:(nn + 1) * 512], onescol_sb,
                                         src[:, k * S + nn * 512: k * S + (nn + 1) * 512],
                                         start=(k == 0), stop=(k == KD - 1))
                            t.wait_ge(s_sc, sc0 + k + 1)
                            ins = None
                            for nn in range(4):
                                ins = t.matmul(
                                    ps_sq[:, nn * 512:(nn + 1) * 512], onescol_bf,
                                    sqbf[:, (k % 2) * S + nn * 512: (k % 2) * S + (nn + 1) * 512],
                                    start=(k == 0), stop=(k == KD - 1))
                            ins.then_inc(s_pe, 1)

                    n["pe"] += KD

                    @blk.vector
                    def _(v):
                        v.wait_ge(s_pe, n["pe"])
                        vop(v, "tensor_scalar_mul", rowm, ps_sx[:], 1.0 / D)
                        vop(v, "tensor_mul", rowu, rowm, rowm)
                        vop(v, "scalar_tensor_tensor", rowr, ps_sq[:], 1.0 / D,
                            rowu, ALU.mult, ALU.subtract)
                        vop(v, "tensor_scalar_add", rowu, rowr, LN_EPS)

                    @blk.scalar
                    def _(sc):
                        sc.wait_ge(s_v, n["v"])
                        scop(sc, "activation", rowt, rowu, AF.Ln)
                        scop(sc, "activation", rowt, rowt, AF.Exp, scale=-0.5)

                    @blk.vector
                    def _(v):
                        v.wait_ge(s_sc, n["sc"])
                        vop(v, "tensor_mul", rowr, rowt, rowt)
                        vop(v, "tensor_mul", rowr, rowr, rowu)
                        vop(v, "tensor_scalar", rowr, rowr, -0.5, 1.5,
                            ALU.mult, ALU.add)
                        vop(v, "tensor_mul", rowr, rowt, rowr)

            with (
                nc.psum_tensor("ps_mrep" + tag, [128, S], F32) as ps_mrep,
                nc.psum_tensor("ps_rrep" + tag, [128, S], F32) as ps_rrep,
            ):
                with nc.Block() as blk:
                    @blk.tensor
                    def _(t):
                        ins = None
                        for nn in range(4):
                            t.matmul(ps_mrep[:, nn * 512:(nn + 1) * 512], ones_sb,
                                     rowm[0:1, nn * 512:(nn + 1) * 512],
                                     start=True, stop=True)
                        for nn in range(4):
                            ins = t.matmul(ps_rrep[:, nn * 512:(nn + 1) * 512], ones_sb,
                                           rowr[0:1, nn * 512:(nn + 1) * 512],
                                           start=True, stop=True)
                        ins.then_inc(s_pe, 1)

                    n["pe"] += 1

                    @blk.vector
                    def _(v):
                        v.wait_ge(s_pe, n["pe"])
                        for k in range(KD):
                            vop(v, "tensor_sub", xsqb[:, (k % 2) * S:(k % 2) * S + S],
                                src[:, k * S:(k + 1) * S], ps_mrep[:])
                            vop(v, "tensor_mul", act_bf[:, k * S:(k + 1) * S],
                                xsqb[:, (k % 2) * S:(k % 2) * S + S], ps_rrep[:])

        with nc.Block() as blk:
            @blk.gpsimd
            def _(g):
                g.dma_start(onescol_bf, onescol_c).then_inc(s_small, 16)
                g.dma_start(sel16_sb, sel16_c).then_inc(s_small, 16)
                g.dma_start(bqs, bq_d).then_inc(s_small, 16)
                g.dma_start(bks, bk_d).then_inc(s_small, 16)
                g.dma_start(bo_sb, bo_d).then_inc(s_small, 16)
                g.dma_start(bvr_sb, bv_d).then_inc(s_small, 16)
                g.dma_start(s1_sb, s1_d).then_inc(s_small, 16)
                g.dma_start(s2_sb, s2_d).then_inc(s_small, 16)
                g.dma_start(sh1_sb, sh1_d).then_inc(s_small, 16)
                g.dma_start(sh2_sb, sh2_d).then_inc(s_small, 16)
                g.dma_start(s3_sb, s3_d).then_inc(s_small, 16)
                g.wait_ge(s_x, 16)
                g.dma_start(wq_sb, wq_d).then_inc(s_wqkv, 16)
                g.dma_start(wk_sb, wk_d).then_inc(s_wqkv, 16)
                g.dma_start(wv_sb, wv_d).then_inc(s_wqkv, 16)
                g.dma_start(wo_sb, wo_d).then_inc(s_wqkv, 16)

            @blk.sync
            def _(sy):
                sy.dma_start(xbuf, xst).then_inc(s_x, 16)
                sy.dma_start(ones_sb, ones_c).then_inc(s_const, 16)
                sy.dma_start(onescol_sb, onescol_c).then_inc(s_const, 16)
                sy.dma_start(id16_sb, id16_c).then_inc(s_const, 16)

            @blk.vector
            def _(v):
                v.wait_ge(s_x, 16)
                v.wait_ge(s_small, 16 * 11)
                vop(v, "tensor_scalar_mul", bqs, bqs, QS)
                vop(v, "tensor_scalar_mul", bks, bks, QS)

            @blk.tensor
            def _(t):
                t.wait_ge(s_x, 16)

        emit_layernorm(xbuf, xsq, sqbf1, rowbuf1, "a")

        with (
            nc.psum_tensor("ps_q", [128, S], F32) as ps_q,
            nc.psum_tensor("ps_k", [128, S], F32) as ps_k,
        ):
            with nc.Block() as blk:
                v0qk = n["v"]
                pe0 = n["pe"]

                @blk.tensor
                def _(t):
                    t.wait_ge(s_wqkv, 16 * 4)
                    for j in range(NPAIR):
                        if j >= 1:
                            t.wait_ge(s_v, v0qk + 2 * (j - 1) + 1)
                        ins = None
                        for k in range(KD):
                            for nn in range(4):
                                ins = t.matmul(
                                    ps_q[:, nn * 512:(nn + 1) * 512],
                                    wq_sb[:, j * 1024 + k * 128: j * 1024 + (k + 1) * 128],
                                    act_bf[:, k * S + nn * 512: k * S + (nn + 1) * 512],
                                    start=(k == 0), stop=(k == KD - 1))
                        ins.then_inc(s_pe, 1)
                        if j >= 1:
                            t.wait_ge(s_v, v0qk + 2 * (j - 1) + 2)
                        for k in range(KD):
                            for nn in range(4):
                                ins = t.matmul(
                                    ps_k[:, nn * 512:(nn + 1) * 512],
                                    wk_sb[:, j * 1024 + k * 128: j * 1024 + (k + 1) * 128],
                                    act_bf[:, k * S + nn * 512: k * S + (nn + 1) * 512],
                                    start=(k == 0), stop=(k == KD - 1))
                        ins.then_inc(s_pe, 1)

                n["pe"] += 2 * NPAIR

                @blk.vector
                def _(v):
                    for j in range(NPAIR):
                        v.wait_ge(s_pe, pe0 + 2 * j + 1)
                        vop(v, "tensor_scalar", q_sb[:, j * S:(j + 1) * S],
                            ps_q[:], QS, bqs[:, j:j + 1], ALU.mult, ALU.add)
                        v.wait_ge(s_pe, pe0 + 2 * j + 2)
                        vop(v, "tensor_scalar", k_sb[:, j * S:(j + 1) * S],
                            ps_k[:], QS, bks[:, j:j + 1], ALU.mult, ALU.add)

        vt3 = vt_sb.rearrange("p (s e) -> p s e", e=65)
        with (
            nc.psum_tensor("ps_v", [128, 2 * 1024], F32) as ps_v,
            nc.psum_tensor("ps_bv", [128, 1024], F32) as ps_bv,
        ):
            with nc.Block() as blk:
                @blk.vector
                def _(v):
                    vop(v, "memset", vt3[:, :, 64:65], 1.0)

                @blk.tensor
                def _(t):
                    t.wait_ge(s_small, 16 * 11)
                    ins = None
                    for nn in range(2):
                        ins = t.matmul(ps_bv[:, nn * 512:(nn + 1) * 512], ones_sb,
                                       bvr_sb[0:1, nn * 512:(nn + 1) * 512],
                                       start=True, stop=True)
                    ins.then_inc(s_pe, 1)

                n["pe"] += 1

                @blk.vector
                def _(v):
                    v.wait_ge(s_pe, n["pe"])
                    vop(v, "tensor_copy", bvrep_sb, ps_bv[:])

            with nc.Block() as blk:
                pe0 = n["pe"]
                v0 = n["v"]

                @blk.tensor
                def _(t):
                    for c in range(TC):
                        if c >= 2:
                            t.wait_ge(s_v, v0 + c - 1)
                        ins = None
                        for k in range(KD):
                            for nn in range(2):
                                ins = t.matmul(
                                    ps_v[:, (c % 2) * 1024 + nn * 512:(c % 2) * 1024 + (nn + 1) * 512],
                                    act_bf[:, k * S + c * 128: k * S + (c + 1) * 128],
                                    wv_sb[:, k * 1024 + nn * 512: k * 1024 + (nn + 1) * 512],
                                    start=(k == 0), stop=(k == KD - 1))
                        ins.then_inc(s_pe, 1)

                n["pe"] += TC

                @blk.vector
                def _(v):
                    for c in range(TC):
                        v.wait_ge(s_pe, pe0 + c + 1)
                        vop(v, "tensor_add",
                            vt3[:, c * 16:(c + 1) * 16, 0:64],
                            ps_v[:, (c % 2) * 1024:(c % 2) * 1024 + 1024]
                            .rearrange("p (h e) -> p h e", e=64),
                            bvrep_sb.rearrange("p (h e) -> p h e", e=64))

        with nc.psum_tensor("ps_n16", [16, S], F32) as ps_n16:
            for which, src_sb, col in (("q", q_sb, 0), ("k", k_sb, 1)):
                with nc.Block() as blk:
                    v0 = n["v"]
                    pe0 = n["pe"]

                    @blk.vector
                    def _(v, src_sb=src_sb, pe0=pe0):
                        for j in range(NPAIR):
                            if j >= 1:
                                v.wait_ge(s_pe, pe0 + j)
                            vop(v, "tensor_mul", q2buf,
                                src_sb[:, j * S:(j + 1) * S],
                                src_sb[:, j * S:(j + 1) * S])

                    @blk.tensor
                    def _(t, v0=v0):
                        for j in range(NPAIR):
                            t.wait_ge(s_v, v0 + j + 1)
                            ins = None
                            for nn in range(4):
                                ins = t.matmul(
                                    ps_n16[:, nn * 512:(nn + 1) * 512],
                                    sel16_sb[:, j * 16:(j + 1) * 16],
                                    q2buf[:, nn * 512:(nn + 1) * 512],
                                    start=(j == 0), stop=(j == NPAIR - 1))
                            ins.then_inc(s_pe, 1)

                    n["pe"] += NPAIR

                    @blk.vector
                    def _(v, col=col):
                        v.wait_ge(s_pe, n["pe"])
                        vop(v, "reduce_max", qm2[:, col:col + 1], ps_n16[:], axis=AX.X)

        with (
            nc.psum_tensor("ps_t16", [1, 16], F32) as ps_t16,
            nc.psum_tensor("ps_crep", [128, 8], F32) as ps_crep,
        ):
            with nc.Block() as blk:
                @blk.vector
                def _(v):
                    vop(v, "tensor_add", qm2[:, 0:1], qm2[:, 0:1], qm2[:, 1:2])

                @blk.tensor
                def _(t):
                    t.wait_ge(s_v, n["v"])
                    t.transpose(ps_t16[:], qm2[:, 0:1], id16_sb).then_inc(s_pe, 1)

                n["pe"] += 1

                @blk.vector
                def _(v):
                    v.wait_ge(s_pe, n["pe"])
                    vop(v, "tensor_copy", crow[:, 16:32], ps_t16[:])
                    vop(v, "tensor_max", crow[:, 0:8],
                        crow[:, 16:32:2], crow[:, 17:32:2])
                    vop(v, "tensor_scalar_mul", crow[:, 0:8], crow[:, 0:8], -0.505)

                @blk.tensor
                def _(t):
                    t.wait_ge(s_v, n["v"])
                    t.matmul(ps_crep[:], ones_sb, crow[0:1, 0:8],
                             start=True, stop=True).then_inc(s_pe, 1)

                n["pe"] += 1

                @blk.vector
                def _(v):
                    v.wait_ge(s_pe, n["pe"])
                    vop(v, "tensor_copy", cneg, ps_crep[:])

        TQ = S // 4
        with (
            nc.psum_tensor("ps_s", [128, 2 * 1024], F32) as ps_s,
            nc.psum_tensor("ps_aE", [65, TQ], F32) as ps_aE,
            nc.psum_tensor("ps_aO", [65, TQ], F32) as ps_aO,
        ):
            exp0 = n["exp"]
            att0 = n["att"]
            scx0 = n["scx"]
            dcp0 = n["dcp"]
            NI = NPAIR * 4
            with nc.Block() as blk:
                @blk.tensor
                def _(t):
                    def emit_att(I, c):
                        j = I // 4
                        t.wait_ge(s_exp, exp0 + I * TC + c + 1)
                        if c == 0 and I >= 1:
                            t.wait_ge(s_dcp, dcp0 + I)
                        sb = (c % 4) * 1024
                        ins = None
                        for hh, ps_a in ((0, ps_aE), (1, ps_aO)):
                            vslot = (c * 16 + 2 * j + hh) * 65
                            ins = t.matmul(
                                ps_a[:, :],
                                vt_sb[:, vslot: vslot + 65],
                                pbuf[:, sb + hh * TQ: sb + (hh + 1) * TQ],
                                start=(c == 0), stop=(c == TC - 1))
                        ins.then_inc(s_att, 1)

                    for I in range(NI):
                        j, qq = I // 4, I % 4
                        for c in range(TC):
                            wv_ = exp0 + I * TC + c - 1
                            if wv_ > 0:
                                t.wait_ge(s_exp, wv_)
                            bufo = (c % 2) * 1024
                            ins = None
                            for hh in range(2):
                                p0 = hh * 64
                                ins = t.matmul(
                                    ps_s[:, bufo + hh * TQ: bufo + (hh + 1) * TQ],
                                    k_sb[p0:p0 + 64, j * S + c * 128: j * S + (c + 1) * 128],
                                    q_sb[p0:p0 + 64, j * S + qq * TQ: j * S + (qq + 1) * TQ],
                                    start=True, stop=True)
                            ins.then_inc(s_scx, 1)
                            if c >= 2:
                                emit_att(I, c - 2)
                        emit_att(I, TC - 2)
                        emit_att(I, TC - 1)

                n["scx"] += NI * TC
                n["att"] += NI * TC

                @blk.scalar
                def _(sc):
                    for I in range(NI):
                        j = I // 4
                        for c in range(TC):
                            idx = I * TC + c
                            sc.wait_ge(s_scx, scx0 + idx + 1)
                            wv_ = att0 + I * TC + c - 3
                            if wv_ > 0:
                                sc.wait_ge(s_att, wv_)
                            sc.activation(pbuf[:, (c % 4) * 1024:(c % 4) * 1024 + 1024],
                                          ps_s[:, (c % 2) * 1024:(c % 2) * 1024 + 1024],
                                          AF.Exp,
                                          bias=cneg[:, j:j + 1]).then_inc(s_exp, 1)
                    n["exp"] += NI * TC

                @blk.vector
                def _(v):
                    for I in range(NI):
                        j, qq = I // 4, I % 4
                        v.wait_ge(s_att, att0 + (I + 1) * TC)
                        for kk, ps_a in ((2 * I, ps_aE), (2 * I + 1, ps_aO)):
                            if kk >= 2:
                                v.wait_ge(sdd[kk % 2], 16 * (kk // 2))
                            vop(v, "tensor_copy",
                                dtmp[64:65, (kk % 2) * 1024:(kk % 2) * 1024 + TQ],
                                ps_a[64:65, :])
                        vop(v, "tensor_copy",
                            act_bf[0:64, j * S + qq * TQ: j * S + (qq + 1) * TQ],
                            ps_aE[0:64, :])
                        v.wait_ge(s_v, n["v"])
                        v.tensor_copy(
                            act_bf[64:128, j * S + qq * TQ: j * S + (qq + 1) * TQ],
                            ps_aO[0:64, :]).then_inc(s_dcp, 1)
                        n["dcp"] += 1

                v3d = n["v"]

                @blk.sync
                def _(sy):
                    sy.dma_start(selpair_sb, selpair_c).then_inc(s_const, 16)
                    vbase = v3d - NI * 3
                    for I in range(NI):
                        j, qq = I // 4, I % 4
                        for kk, row in ((2 * I, 2 * j), (2 * I + 1, 2 * j + 1)):
                            sy.wait_ge(s_v, vbase + I * 3 + (kk % 2) + 1)
                            if kk >= 2:
                                sy.wait_ge(sdd[kk % 2], 16 * (kk // 2))
                            sy.dma_start(
                                denom_sb[row:row + 1, qq * TQ:(qq + 1) * TQ],
                                dtmp[64:65, (kk % 2) * 1024:(kk % 2) * 1024 + TQ]
                            ).then_inc(sdd[kk % 2], 16)

        with nc.psum_tensor("ps_rep", [128, 2 * S], F32) as ps_rep:
            with nc.Block() as blk:
                @blk.vector
                def _(v):
                    v.wait_ge(s_dd0, 16 * 32)
                    v.wait_ge(s_dd1, 16 * 32)
                    vop(v, "reciprocal", rtile, denom_sb)

                v0 = n["v"]
                pe0 = n["pe"]

                @blk.tensor
                def _(t):
                    t.wait_ge(s_v, v0)
                    t.wait_ge(s_const, 64)
                    for j in range(NPAIR):
                        if j >= 2:
                            t.wait_ge(s_v, v0 + j - 1)
                        ins = None
                        for nn in range(4):
                            ins = t.matmul(
                                ps_rep[:, (j % 2) * S + nn * 512:(j % 2) * S + (nn + 1) * 512],
                                selpair_sb[:, j * 128:(j + 1) * 128],
                                rtile[:, nn * 512:(nn + 1) * 512],
                                start=True, stop=True)
                        ins.then_inc(s_pe, 1)

                n["pe"] += NPAIR

                @blk.vector
                def _(v):
                    for j in range(NPAIR):
                        v.wait_ge(s_pe, pe0 + j + 1)
                        vop(v, "tensor_mul", attn_n[:, j * S:(j + 1) * S],
                            act_bf[:, j * S:(j + 1) * S],
                            ps_rep[:, (j % 2) * S:(j % 2) * S + S])

        with nc.psum_tensor("ps_wo", [128, 2 * S], F32) as ps_wo:
            with nc.Block() as blk:
                v0 = n["v"]
                pe0 = n["pe"]

                @blk.sync
                def _(sy):
                    for m in range(KD):
                        if m >= 2:
                            sy.wait_ge(s_v, v0 + m - 1)
                        sy.dma_start(xrb[:, (m % 2) * S:(m % 2) * S + S],
                                     xst[:, m * S:(m + 1) * S]).then_inc(sxr[m % 2], 16)

                @blk.tensor
                def _(t):
                    for m in range(KD):
                        if m >= 2:
                            t.wait_ge(s_v, v0 + m - 1)
                        ins = None
                        for j in range(NPAIR):
                            for nn in range(4):
                                ins = t.matmul(
                                    ps_wo[:, (m % 2) * S + nn * 512:(m % 2) * S + (nn + 1) * 512],
                                    wo_sb[:, m * 1024 + j * 128: m * 1024 + (j + 1) * 128],
                                    attn_n[:, j * S + nn * 512: j * S + (nn + 1) * 512],
                                    start=(j == 0), stop=(j == NPAIR - 1))
                        ins.then_inc(s_pe, 1)

                n["pe"] += KD

                @blk.vector
                def _(v):
                    for m in range(KD):
                        v.wait_ge(s_pe, pe0 + m + 1)
                        v.wait_ge(sxr[m % 2], 16 * (m // 2 + 1))
                        vop(v, "scalar_tensor_tensor",
                            x2buf[:, m * S:(m + 1) * S],
                            ps_wo[:, (m % 2) * S:(m % 2) * S + S],
                            bo_sb[:, m:m + 1],
                            xrb[:, (m % 2) * S:(m % 2) * S + S],
                            ALU.add, ALU.add)

                @blk.sync
                def _(sy):
                    for m in range(KD):
                        sy.wait_ge(s_v, v0 + m + 1)
                        sy.dma_start(x2hbm[:, m * S:(m + 1) * S],
                                     x2buf[:, m * S:(m + 1) * S]).then_inc(s_x2st, 16)

        emit_layernorm(x2buf, xsq2, sqbf2, rowbuf2, "b")

        wb = [wb0, wb1]
        sched = []
        for hf in range(2):
            sched += [("w1", hf, m) for m in range(KH)]
            sched += [("w2", hf, m) for m in range(KH)]
            sched += [("h1", hf, m) for m in range(KH)]
            sched += [("h2", hf, m) for m in range(KH)]
            sched += [("w3", hf, m) for m in range(KD)]
        NS = len(sched)
        wsrc = {"w1": (w1_d, KD), "w2": (w2_d, KD), "h1": (h1_d, KH),
                "h2": (h2_d, KH), "w3": (w3_d, KH)}
        eng = {"w1": "s", "w2": "v", "h1": "s", "h2": "s", "w3": "v"}
        evs_b = [0] * (NS + 1)
        evv_b = [0] * (NS + 1)
        for i, (lay, hf, m) in enumerate(sched):
            evs_b[i + 1] = evs_b[i] + (1 if eng[lay] == "s" else 0)
            evv_b[i + 1] = evv_b[i] + (1 if eng[lay] == "v" else 0)

        mm0 = n["mm"]
        wc0 = n["wc"]
        evs0 = n["evs"]
        evv0 = n["evv"]
        w3_evv_val = [evv0 + evv_b[i] + 1
                      for i, (lay, hf, m) in enumerate(sched) if lay == "w3"]

        with nc.psum_tensor("ps_f", [128, 2 * 1024], F32) as ps_f:
            with nc.Block() as blk:
                @blk.gpsimd
                def _(g):
                    for i, (lay, hf, m) in enumerate(sched):
                        src, kc = wsrc[lay]
                        if i >= 2:
                            g.wait_ge(s_mm, mm0 + i - 1)
                        g.dma_start(wb[i % 2][:, 0: kc * 128], src[m]).then_inc(swl[i % 2], 16)

                @blk.tensor
                def _(t):
                    for i, (lay, hf, m) in enumerate(sched):
                        src, kc = wsrc[lay]
                        t.wait_ge(swl[i % 2], 16 * (i // 2 + 1))
                        if m == 0:
                            if evs_b[i] > 0:
                                t.wait_ge(s_evs, evs0 + evs_b[i])
                            if evv_b[i] > 0:
                                t.wait_ge(s_evv, evv0 + evv_b[i])
                        elif i >= 2:
                            lay2 = sched[i - 2][0]
                            if eng[lay2] == "s":
                                t.wait_ge(s_evs, evs0 + evs_b[i - 2] + 1)
                            else:
                                t.wait_ge(s_evv, evv0 + evv_b[i - 2] + 1)
                        if lay in ("w1", "w2"):
                            rhs, rb, rstride = act_bf, hf * THALF, S
                        elif lay == "h1":
                            rhs, rb, rstride = hbufB, 0, THALF
                        elif lay == "h2":
                            rhs, rb, rstride = hbufA, 0, THALF
                        else:
                            rhs, rb, rstride = hbufB, 0, THALF
                        ins = None
                        for k in range(kc):
                            for nn in range(2):
                                ins = t.matmul(
                                    ps_f[:, (i % 2) * 1024 + nn * 512:(i % 2) * 1024 + (nn + 1) * 512],
                                    wb[i % 2][:, k * 128:(k + 1) * 128],
                                    rhs[:, k * rstride + rb + nn * 512: k * rstride + rb + (nn + 1) * 512],
                                    start=(k == 0), stop=(k == kc - 1))
                        ins.then_inc(s_mm, 1)

                n["wc"] += NS
                n["mm"] += NS

                @blk.scalar
                def _(sc):
                    sc.wait_ge(s_x2st, 16 * KD)
                    for i, (lay, hf, m) in enumerate(sched):
                        if eng[lay] != "s":
                            continue
                        sc.wait_ge(s_mm, mm0 + i + 1)
                        dst = hbufA if lay in ("w1", "h1") else hbufB
                        func = AF.Silu if lay == "w1" else AF.Gelu
                        ssb = {"w1": s1_sb, "h1": sh1_sb, "h2": sh2_sb}[lay]
                        sc.activation(dst[:, m * THALF:(m + 1) * THALF],
                                      ps_f[:, (i % 2) * 1024:(i % 2) * 1024 + 1024],
                                      func, scale=ssb[:, m:m + 1]).then_inc(s_evs, 1)
                        n["evs"] += 1

                @blk.vector
                def _(v):
                    v.wait_ge(s_x2st, 16 * KD)
                    nxr = 0
                    nst = 0
                    for i, (lay, hf, m) in enumerate(sched):
                        if eng[lay] != "v":
                            continue
                        v.wait_ge(s_mm, mm0 + i + 1)
                        v.wait_ge(s_v, n["v"])
                        if lay == "w2":
                            v.scalar_tensor_tensor(
                                hbufB[:, m * THALF:(m + 1) * THALF],
                                ps_f[:, (i % 2) * 1024:(i % 2) * 1024 + 1024],
                                s2_sb[:, m:m + 1],
                                hbufA[:, m * THALF:(m + 1) * THALF],
                                ALU.mult, ALU.mult).then_inc(s_evv, 1)
                            n["evv"] += 1
                        else:
                            v.wait_ge(sxr[nxr % 2], 16 * (KD // 2) + 16 * (nxr // 2 + 1))
                            if nst >= 2:
                                v.wait_ge(sout[nst % 2], 16 * (nst // 2))
                            v.scalar_tensor_tensor(
                                ostg[:, (nst % 2) * 1024:(nst % 2) * 1024 + 1024],
                                ps_f[:, (i % 2) * 1024:(i % 2) * 1024 + 1024],
                                s3_sb[:, m:m + 1],
                                x2rb[:, (nxr % 2) * 1024:(nxr % 2) * 1024 + 1024],
                                ALU.mult, ALU.add).then_inc(s_evv, 1)
                            n["evv"] += 1
                            nxr += 1
                            nst += 1

                @blk.sync
                def _(sy):
                    nxr = 0
                    nst = 0
                    for i, (lay, hf, m) in enumerate(sched):
                        if lay != "w3":
                            continue
                        if nxr == 0:
                            sy.wait_ge(s_x2st, 16 * KD)
                        if nxr >= 2:
                            sy.wait_ge(s_evv, w3_evv_val[nxr - 2])
                        sy.dma_start(x2rb[:, (nxr % 2) * 1024:(nxr % 2) * 1024 + 1024],
                                     x2hbm[:, m * S + hf * 1024: m * S + hf * 1024 + 1024]
                                     ).then_inc(sxr[nxr % 2], 16)
                        nxr += 1
                        sy.wait_ge(s_evv, w3_evv_val[nst])
                        sy.dma_start(out_fm[:, m * S + hf * 1024: m * S + hf * 1024 + 1024],
                                     ostg[:, (nst % 2) * 1024:(nst % 2) * 1024 + 1024]
                                     ).then_inc(sout[nst % 2], 16)
                        nst += 1
                    sy.wait_ge(sout[0], 16 * 8)
                    sy.wait_ge(sout[1], 16 * 8)

    return nc


def kernel(**inputs):
    ins = inputs
    x = np.asarray(ins["L0_attn_x"], np.float32)

    def stage_w(wmat):
        WT = np.ascontiguousarray(np.asarray(wmat, np.float32).T)
        kin, kout = WT.shape
        KC, MT = kin // 128, kout // 128
        return np.ascontiguousarray(
            WT.reshape(KC, 128, MT, 128).transpose(2, 1, 0, 3).reshape(MT, 128, KC * 128))

    wq_m = np.asarray(ins["L0_attn_wq"], np.float32).transpose(1, 0, 2).reshape(D, D)
    wk_m = np.asarray(ins["L0_attn_wk"], np.float32).transpose(1, 0, 2).reshape(D, D)
    wv_m = np.asarray(ins["L0_attn_wv"], np.float32).transpose(1, 0, 2).reshape(D, D)
    wq_st = np.ascontiguousarray(
        wq_m.reshape(KD, 128, NPAIR, 128).transpose(1, 2, 0, 3).reshape(128, 8192))
    wk_st = np.ascontiguousarray(
        wk_m.reshape(KD, 128, NPAIR, 128).transpose(1, 2, 0, 3).reshape(128, 8192))
    wv_st = np.ascontiguousarray(
        wv_m.reshape(KD, 128, D).transpose(1, 0, 2).reshape(128, 8192))
    woT = np.ascontiguousarray(np.asarray(ins["L0_attn_wo"], np.float32).T)
    wo_st = np.ascontiguousarray(
        woT.reshape(KD, 128, KD, 128).transpose(1, 2, 0, 3).reshape(128, 8192))

    bq_st = np.ascontiguousarray(
        np.asarray(ins["L0_attn_bq"], np.float32).reshape(NPAIR, 2, 64)
        .transpose(1, 2, 0).reshape(128, NPAIR))
    bk_st = np.ascontiguousarray(
        np.asarray(ins["L0_attn_bk"], np.float32).reshape(NPAIR, 2, 64)
        .transpose(1, 2, 0).reshape(128, NPAIR))
    bv_st = np.ascontiguousarray(np.asarray(ins["L0_attn_bv"], np.float32).reshape(1, D))
    bo_st = np.ascontiguousarray(np.asarray(ins["L0_attn_bo"], np.float32).reshape(KD, 128).T)

    w1_st = stage_w(ins["L0_ffn_w1_w"])
    w2_st = stage_w(ins["L0_ffn_w2_w"])
    h1_st = stage_w(ins["L0_ffn_h1_w"])
    h2_st = stage_w(ins["L0_ffn_h2_w"])
    w3_st = stage_w(ins["L0_ffn_w3_w"])
    s1_st = np.ascontiguousarray(np.asarray(ins["L0_ffn_w1_s"], np.float32).reshape(KH, 128).T)
    s2_st = np.ascontiguousarray(np.asarray(ins["L0_ffn_w2_s"], np.float32).reshape(KH, 128).T)
    sh1_st = np.ascontiguousarray(np.asarray(ins["L0_ffn_h1_s"], np.float32).reshape(KH, 128).T)
    sh2_st = np.ascontiguousarray(np.asarray(ins["L0_ffn_h2_s"], np.float32).reshape(KH, 128).T)
    s3_st = np.ascontiguousarray(np.asarray(ins["L0_ffn_w3_s"], np.float32).reshape(KD, 128).T)

    nc = build_nc()
    in_maps = []
    for c in range(B):
        xc = np.ascontiguousarray(
            x[c].T.reshape(KD, 128, S).transpose(1, 0, 2).reshape(128, KD * S))
        in_maps.append({
            "xst": xc, "wq": wq_st, "wk": wk_st, "wv": wv_st, "wo": wo_st,
            "bq": bq_st, "bk": bk_st, "bvr": bv_st, "bo": bo_st,
            "w1": w1_st, "w2": w2_st, "h1": h1_st, "h2": h2_st, "w3": w3_st,
            "s1": s1_st, "s2": s2_st, "sh1": sh1_st, "sh2": sh2_st, "s3": s3_st,
        })
    res = run_bass_kernel_spmd(nc, in_maps, core_ids=list(range(B)))
    outs = []
    for c in range(B):
        o = np.asarray(res.results[c]["out_fm"], np.float32)
        outs.append(o.reshape(128, KD, S).transpose(1, 0, 2).reshape(D, S).T)
    return np.ascontiguousarray(np.stack(outs, 0))


# version 12
# speedup vs baseline: 1.0102x; 1.0102x over previous
import numpy as np
from contextlib import ExitStack
import concourse.bass as bass
import concourse.mybir as mybir
from concourse.bass_utils import run_bass_kernel_spmd

F32 = mybir.dt.float32
BF16 = mybir.dt.bfloat16
AF = mybir.ActivationFunctionType
ALU = mybir.AluOpType
AX = mybir.AxisListType

B, S, D, H, E, HID = 8, 2048, 1024, 16, 64, 4096
KD = D // 128
KH = HID // 128
NPAIR = H // 2
TC = S // 128
QS = float(8.0 ** -0.5)
LN_EPS = 1e-5
THALF = S // 2

AR = {
    "xbuf":    (0,      65536),
    "xsq":     (65536,  16384),
    "rowbuf1": (81920,  16384),
    "sqbf1":   (98304,  8192),
    "wvld":    (106496, 16384),
    "wkld":    (122880, 16384),
    "wqld":    (139264, 16384),
    "wo_sb":   (155648, 16384),
    "q_sb":    (0,      32768),
    "k_sb":    (32768,  32768),
    "vt_sb":   (65536,  33280),
    "bvrep":   (98816,  4096),
    "q2buf":   (102912, 4096),
    "qm2":     (139264, 256),
    "crow":    (139520, 128),
    "denom":   (106496, 8192),
    "selpair": (98816,  4096),
    "pbuf":    (122880, 8192),
    "dtmp":    (131072, 8192),
    "attn_n":  (0,      32768),
    "x2buf":   (32768,  65536),
    "rtile":   (114688, 8192),
    "xrb":     (122880, 16384),
    "xsq2":    (0,      16384),
    "rowbuf2": (16384,  16384),
    "sqbf2":   (98304,  8192),
    "hbufA":   (0,      65536),
    "hbufB":   (65536,  65536),
    "wb0":     (139264, 8192),
    "wb1":     (147456, 8192),
    "x2rb":    (131072, 8192),
    "ostg":    (155648, 8192),
}
ARENA_SIZE = 172032


def build_nc():
    nc = bass.Bass("TRN2", target_bir_lowering=False)

    xst = nc.dram_tensor("xst", [128, KD * S], F32, kind="ExternalInput").ap()
    wq_d = nc.dram_tensor("wq", [128, 8192], F32, kind="ExternalInput").ap()
    wk_d = nc.dram_tensor("wk", [128, 8192], F32, kind="ExternalInput").ap()
    wv_d = nc.dram_tensor("wv", [128, 8192], F32, kind="ExternalInput").ap()
    wo_d = nc.dram_tensor("wo", [128, 8192], F32, kind="ExternalInput").ap()
    bq_d = nc.dram_tensor("bq", [128, 8], F32, kind="ExternalInput").ap()
    bk_d = nc.dram_tensor("bk", [128, 8], F32, kind="ExternalInput").ap()
    bv_d = nc.dram_tensor("bvr", [1, 1024], F32, kind="ExternalInput").ap()
    bo_d = nc.dram_tensor("bo", [128, 8], F32, kind="ExternalInput").ap()
    w1_d = nc.dram_tensor("w1", [KH, 128, KD * 128], F32, kind="ExternalInput").ap()
    w2_d = nc.dram_tensor("w2", [KH, 128, KD * 128], F32, kind="ExternalInput").ap()
    h1_d = nc.dram_tensor("h1", [KH, 128, KH * 128], F32, kind="ExternalInput").ap()
    h2_d = nc.dram_tensor("h2", [KH, 128, KH * 128], F32, kind="ExternalInput").ap()
    w3_d = nc.dram_tensor("w3", [KD, 128, KH * 128], F32, kind="ExternalInput").ap()
    s1_d = nc.dram_tensor("s1", [128, KH], F32, kind="ExternalInput").ap()
    s2_d = nc.dram_tensor("s2", [128, KH], F32, kind="ExternalInput").ap()
    sh1_d = nc.dram_tensor("sh1", [128, KH], F32, kind="ExternalInput").ap()
    sh2_d = nc.dram_tensor("sh2", [128, KH], F32, kind="ExternalInput").ap()
    s3_d = nc.dram_tensor("s3", [128, KD], F32, kind="ExternalInput").ap()
    out_fm = nc.dram_tensor("out_fm", [128, KD * S], F32, kind="ExternalOutput").ap()
    x2hbm = nc.dram_tensor("x2hbm", [128, KD * S], F32, kind="Internal").ap()

    ones_c = nc.inline_tensor(np.ones((1, 128), np.float32), "ones_c").ap()
    onescol_c = nc.inline_tensor(np.ones((128, 1), np.float32), "onescol_c").ap()
    sel = np.zeros((128, NPAIR * 16), np.float32)
    for j in range(NPAIR):
        sel[0:64, j * 16 + 2 * j] = 1.0
        sel[64:128, j * 16 + 2 * j + 1] = 1.0
    sel16_c = nc.inline_tensor(sel, "sel16_c").ap()
    sp = np.zeros((16, NPAIR * 128), np.float32)
    for j in range(NPAIR):
        sp[2 * j, j * 128: j * 128 + 64] = 1.0
        sp[2 * j + 1, j * 128 + 64: j * 128 + 128] = 1.0
    selpair_c = nc.inline_tensor(sp, "selpair_c").ap()
    id16_c = nc.inline_tensor(np.eye(16, dtype=np.float32), "id16_c").ap()

    ones_sb = nc.alloc_sbuf_tensor("ones_sb", [1, 128], F32)[:]
    onescol_sb = nc.alloc_sbuf_tensor("onescol_sb", [128, 1], F32)[:]
    onescol_bf = nc.alloc_sbuf_tensor("onescol_bf", [128, 1], BF16)[:]
    id16_sb = nc.alloc_sbuf_tensor("id16_sb", [16, 16], F32)[:]
    sel16_sb = nc.alloc_sbuf_tensor("sel16_sb", [128, NPAIR * 16], BF16)[:]
    bqs = nc.alloc_sbuf_tensor("bqs", [128, 8], F32)[:]
    bks = nc.alloc_sbuf_tensor("bks", [128, 8], F32)[:]
    bo_sb = nc.alloc_sbuf_tensor("bo_sb", [128, 8], F32)[:]
    bvr_sb = nc.alloc_sbuf_tensor("bvr_sb", [1, 1024], F32)[:]
    s1_sb = nc.alloc_sbuf_tensor("s1_sb", [128, KH], F32)[:]
    s2_sb = nc.alloc_sbuf_tensor("s2_sb", [128, KH], F32)[:]
    sh1_sb = nc.alloc_sbuf_tensor("sh1_sb", [128, KH], F32)[:]
    sh2_sb = nc.alloc_sbuf_tensor("sh2_sb", [128, KH], F32)[:]
    s3_sb = nc.alloc_sbuf_tensor("s3_sb", [128, KD], F32)[:]
    cneg = nc.alloc_sbuf_tensor("cneg", [128, NPAIR], F32)[:]
    act_bf = nc.alloc_sbuf_tensor("act_bf", [128, KD * S], BF16)[:]

    arena_base = (nc.sbuf_base + 255) & ~255
    assert arena_base + ARENA_SIZE <= nc.sbuf_top, (
        f"arena overflow: base={arena_base} size={ARENA_SIZE} top={nc.sbuf_top}")

    def arena(name, shape, dtype):
        off, sz = AR[name]
        esz = 2 if dtype == BF16 else 4
        nbytes = int(np.prod(shape[1:])) * esz
        assert nbytes <= sz, (name, nbytes, sz)
        return nc.alloc_sbuf_tensor_at(name, shape, dtype,
                                       offset=arena_base + off)[:]

    xbuf = arena("xbuf", [128, KD * S], F32)
    xsq = arena("xsq", [128, 2 * S], F32)
    sqbf1 = arena("sqbf1", [128, 2 * S], BF16)
    rowbuf1 = arena("rowbuf1", [1, 2 * S], F32)
    wv_sb = arena("wvld", [128, 8192], BF16)
    wk_sb = arena("wkld", [128, 8192], BF16)
    wq_sb = arena("wqld", [128, 8192], BF16)
    q_sb = arena("q_sb", [128, NPAIR * S], BF16)
    k_sb = arena("k_sb", [128, NPAIR * S], BF16)
    vt_sb = arena("vt_sb", [128, TC * 16 * 65], BF16)
    bvrep_sb = arena("bvrep", [128, 1024], F32)
    q2buf = arena("q2buf", [128, S], BF16)
    qm2 = arena("qm2", [16, 2], F32)
    crow = arena("crow", [1, 32], F32)
    denom_sb = arena("denom", [16, S], F32)
    selpair_sb = arena("selpair", [16, NPAIR * 128], F32)
    pbuf = arena("pbuf", [128, 2 * S], BF16)
    dtmp = arena("dtmp", [65, 2 * 1024], F32)
    wo_sb = arena("wo_sb", [128, 8192], BF16)
    attn_n = arena("attn_n", [128, NPAIR * S], BF16)
    x2buf = arena("x2buf", [128, KD * S], F32)
    rtile = arena("rtile", [16, S], F32)
    xrb = arena("xrb", [128, 2 * S], F32)
    xsq2 = arena("xsq2", [128, 2 * S], F32)
    sqbf2 = arena("sqbf2", [128, 2 * S], BF16)
    rowbuf2 = arena("rowbuf2", [1, 2 * S], F32)
    hbufA = arena("hbufA", [128, KH * THALF], BF16)
    hbufB = arena("hbufB", [128, KH * THALF], BF16)
    wb0 = arena("wb0", [128, KH * 128], BF16)
    wb1 = arena("wb1", [128, KH * 128], BF16)
    x2rb = arena("x2rb", [128, 2 * 1024], F32)
    ostg = arena("ostg", [128, 2 * 1024], F32)

    n = {"v": 0, "sc": 0, "pe": 0, "exp": 0, "att": 0, "scx": 0, "dcp": 0,
         "wc": 0, "mm": 0, "evs": 0, "evv": 0}

    es = ExitStack()
    with es:
        s_x = es.enter_context(nc.semaphore("s_x"))
        s_const = es.enter_context(nc.semaphore("s_const"))
        s_wqkv = es.enter_context(nc.semaphore("s_wqkv"))
        s_small = es.enter_context(nc.semaphore("s_small"))
        s_v = es.enter_context(nc.semaphore("s_v"))
        s_sc = es.enter_context(nc.semaphore("s_sc"))
        s_pe = es.enter_context(nc.semaphore("s_pe"))
        s_exp = es.enter_context(nc.semaphore("s_exp"))
        s_att = es.enter_context(nc.semaphore("s_att"))
        s_scx = es.enter_context(nc.semaphore("s_scx"))
        s_dcp = es.enter_context(nc.semaphore("s_dcp"))
        s_wl0 = es.enter_context(nc.semaphore("s_wl0"))
        s_wl1 = es.enter_context(nc.semaphore("s_wl1"))
        s_wc = es.enter_context(nc.semaphore("s_wc"))
        s_mm = es.enter_context(nc.semaphore("s_mm"))
        s_evs = es.enter_context(nc.semaphore("s_evs"))
        s_evv = es.enter_context(nc.semaphore("s_evv"))
        s_xr0 = es.enter_context(nc.semaphore("s_xr0"))
        s_xr1 = es.enter_context(nc.semaphore("s_xr1"))
        s_o0 = es.enter_context(nc.semaphore("s_o0"))
        s_o1 = es.enter_context(nc.semaphore("s_o1"))
        s_x2st = es.enter_context(nc.semaphore("s_x2st"))
        s_dd0 = es.enter_context(nc.semaphore("s_dd0"))
        s_dd1 = es.enter_context(nc.semaphore("s_dd1"))

        swl = [s_wl0, s_wl1]
        sxr = [s_xr0, s_xr1]
        sout = [s_o0, s_o1]
        sdd = [s_dd0, s_dd1]

        def vop(v, meth, *a, **kw):
            v.wait_ge(s_v, n["v"])
            getattr(v, meth)(*a, **kw).then_inc(s_v, 1)
            n["v"] += 1
            return n["v"]

        def scop(sc, meth, *a, **kw):
            sc.wait_ge(s_sc, n["sc"])
            getattr(sc, meth)(*a, **kw).then_inc(s_sc, 1)
            n["sc"] += 1
            return n["sc"]

        def emit_layernorm(src, xsqb, sqbf, rowb, tag):
            rowm = rowb[0:1, 0:S]
            rowr = rowb[0:1, S:2 * S]
            rowt = xsqb[0:1, 0:S]
            rowu = xsqb[0:1, S:2 * S]
            with (
                nc.psum_tensor("ps_sx" + tag, [1, S], F32) as ps_sx,
                nc.psum_tensor("ps_sq" + tag, [1, S], F32) as ps_sq,
            ):
                with nc.Block() as blk:
                    pe0ln = n["pe"]

                    @blk.scalar
                    def _(sc):
                        for k in range(KD):
                            if k >= 2:
                                sc.wait_ge(s_pe, pe0ln + k - 1)
                            scop(sc, "activation", sqbf[:, (k % 2) * S:(k % 2) * S + S],
                                 src[:, k * S:(k + 1) * S], AF.Square)

                    sc0 = n["sc"] - KD

                    @blk.tensor
                    def _(t):
                        t.wait_ge(s_const, 48)
                        t.wait_ge(s_small, 16 * 11)
                        for k in range(KD):
                            for nn in range(4):
                                t.matmul(ps_sx[:, nn * 512:(nn + 1) * 512], onescol_sb,
                                         src[:, k * S + nn * 512: k * S + (nn + 1) * 512],
                                         start=(k == 0), stop=(k == KD - 1))
                            t.wait_ge(s_sc, sc0 + k + 1)
                            ins = None
                            for nn in range(4):
                                ins = t.matmul(
                                    ps_sq[:, nn * 512:(nn + 1) * 512], onescol_bf,
                                    sqbf[:, (k % 2) * S + nn * 512: (k % 2) * S + (nn + 1) * 512],
                                    start=(k == 0), stop=(k == KD - 1))
                            ins.then_inc(s_pe, 1)

                    n["pe"] += KD

                    @blk.vector
                    def _(v):
                        v.wait_ge(s_pe, n["pe"])
                        vop(v, "tensor_scalar_mul", rowm, ps_sx[:], 1.0 / D)
                        vop(v, "tensor_mul", rowu, rowm, rowm)
                        vop(v, "scalar_tensor_tensor", rowr, ps_sq[:], 1.0 / D,
                            rowu, ALU.mult, ALU.subtract)
                        vop(v, "tensor_scalar_add", rowu, rowr, LN_EPS)

                    @blk.scalar
                    def _(sc):
                        sc.wait_ge(s_v, n["v"])
                        scop(sc, "activation", rowt, rowu, AF.Ln)
                        scop(sc, "activation", rowt, rowt, AF.Exp, scale=-0.5)

                    @blk.vector
                    def _(v):
                        v.wait_ge(s_sc, n["sc"])
                        vop(v, "tensor_mul", rowr, rowt, rowt)
                        vop(v, "tensor_mul", rowr, rowr, rowu)
                        vop(v, "tensor_scalar", rowr, rowr, -0.5, 1.5,
                            ALU.mult, ALU.add)
                        vop(v, "tensor_mul", rowr, rowt, rowr)

            with (
                nc.psum_tensor("ps_mrep" + tag, [128, S], F32) as ps_mrep,
                nc.psum_tensor("ps_rrep" + tag, [128, S], F32) as ps_rrep,
            ):
                with nc.Block() as blk:
                    @blk.tensor
                    def _(t):
                        ins = None
                        for nn in range(4):
                            t.matmul(ps_mrep[:, nn * 512:(nn + 1) * 512], ones_sb,
                                     rowm[0:1, nn * 512:(nn + 1) * 512],
                                     start=True, stop=True)
                        for nn in range(4):
                            ins = t.matmul(ps_rrep[:, nn * 512:(nn + 1) * 512], ones_sb,
                                           rowr[0:1, nn * 512:(nn + 1) * 512],
                                           start=True, stop=True)
                        ins.then_inc(s_pe, 1)

                    n["pe"] += 1

                    @blk.vector
                    def _(v):
                        v.wait_ge(s_pe, n["pe"])
                        for k in range(KD):
                            vop(v, "tensor_sub", xsqb[:, (k % 2) * S:(k % 2) * S + S],
                                src[:, k * S:(k + 1) * S], ps_mrep[:])
                            vop(v, "tensor_mul", act_bf[:, k * S:(k + 1) * S],
                                xsqb[:, (k % 2) * S:(k % 2) * S + S], ps_rrep[:])

        with nc.Block() as blk:
            @blk.gpsimd
            def _(g):
                g.dma_start(onescol_bf, onescol_c).then_inc(s_small, 16)
                g.dma_start(sel16_sb, sel16_c).then_inc(s_small, 16)
                g.dma_start(bqs, bq_d).then_inc(s_small, 16)
                g.dma_start(bks, bk_d).then_inc(s_small, 16)
                g.dma_start(bo_sb, bo_d).then_inc(s_small, 16)
                g.dma_start(bvr_sb, bv_d).then_inc(s_small, 16)
                g.dma_start(s1_sb, s1_d).then_inc(s_small, 16)
                g.dma_start(s2_sb, s2_d).then_inc(s_small, 16)
                g.dma_start(sh1_sb, sh1_d).then_inc(s_small, 16)
                g.dma_start(sh2_sb, sh2_d).then_inc(s_small, 16)
                g.dma_start(s3_sb, s3_d).then_inc(s_small, 16)
                g.wait_ge(s_x, 16)
                g.dma_start(wq_sb, wq_d).then_inc(s_wqkv, 16)
                g.dma_start(wk_sb, wk_d).then_inc(s_wqkv, 16)
                g.dma_start(wv_sb, wv_d).then_inc(s_wqkv, 16)
                g.dma_start(wo_sb, wo_d).then_inc(s_wqkv, 16)

            @blk.sync
            def _(sy):
                sy.dma_start(xbuf, xst).then_inc(s_x, 16)
                sy.dma_start(ones_sb, ones_c).then_inc(s_const, 16)
                sy.dma_start(onescol_sb, onescol_c).then_inc(s_const, 16)
                sy.dma_start(id16_sb, id16_c).then_inc(s_const, 16)

            @blk.vector
            def _(v):
                v.wait_ge(s_x, 16)
                v.wait_ge(s_small, 16 * 11)
                vop(v, "tensor_scalar_mul", bqs, bqs, QS)
                vop(v, "tensor_scalar_mul", bks, bks, QS)

            @blk.tensor
            def _(t):
                t.wait_ge(s_x, 16)

        emit_layernorm(xbuf, xsq, sqbf1, rowbuf1, "a")

        with (
            nc.psum_tensor("ps_q", [128, S], F32) as ps_q,
            nc.psum_tensor("ps_k", [128, S], F32) as ps_k,
        ):
            with nc.Block() as blk:
                v0qk = n["v"]
                pe0 = n["pe"]

                @blk.tensor
                def _(t):
                    t.wait_ge(s_wqkv, 16 * 4)
                    for j in range(NPAIR):
                        if j >= 1:
                            t.wait_ge(s_v, v0qk + 2 * (j - 1) + 1)
                        ins = None
                        for k in range(KD):
                            for nn in range(4):
                                ins = t.matmul(
                                    ps_q[:, nn * 512:(nn + 1) * 512],
                                    wq_sb[:, j * 1024 + k * 128: j * 1024 + (k + 1) * 128],
                                    act_bf[:, k * S + nn * 512: k * S + (nn + 1) * 512],
                                    start=(k == 0), stop=(k == KD - 1))
                        ins.then_inc(s_pe, 1)
                        if j >= 1:
                            t.wait_ge(s_v, v0qk + 2 * (j - 1) + 2)
                        for k in range(KD):
                            for nn in range(4):
                                ins = t.matmul(
                                    ps_k[:, nn * 512:(nn + 1) * 512],
                                    wk_sb[:, j * 1024 + k * 128: j * 1024 + (k + 1) * 128],
                                    act_bf[:, k * S + nn * 512: k * S + (nn + 1) * 512],
                                    start=(k == 0), stop=(k == KD - 1))
                        ins.then_inc(s_pe, 1)

                n["pe"] += 2 * NPAIR

                @blk.vector
                def _(v):
                    for j in range(NPAIR):
                        v.wait_ge(s_pe, pe0 + 2 * j + 1)
                        vop(v, "tensor_scalar", q_sb[:, j * S:(j + 1) * S],
                            ps_q[:], QS, bqs[:, j:j + 1], ALU.mult, ALU.add)
                        v.wait_ge(s_pe, pe0 + 2 * j + 2)
                        vop(v, "tensor_scalar", k_sb[:, j * S:(j + 1) * S],
                            ps_k[:], QS, bks[:, j:j + 1], ALU.mult, ALU.add)

        vt3 = vt_sb.rearrange("p (s e) -> p s e", e=65)
        with (
            nc.psum_tensor("ps_v", [128, 2 * 1024], F32) as ps_v,
            nc.psum_tensor("ps_bv", [128, 1024], F32) as ps_bv,
        ):
            with nc.Block() as blk:
                @blk.vector
                def _(v):
                    vop(v, "memset", vt3[:, :, 64:65], 1.0)

                @blk.tensor
                def _(t):
                    t.wait_ge(s_small, 16 * 11)
                    ins = None
                    for nn in range(2):
                        ins = t.matmul(ps_bv[:, nn * 512:(nn + 1) * 512], ones_sb,
                                       bvr_sb[0:1, nn * 512:(nn + 1) * 512],
                                       start=True, stop=True)
                    ins.then_inc(s_pe, 1)

                n["pe"] += 1

                @blk.vector
                def _(v):
                    v.wait_ge(s_pe, n["pe"])
                    vop(v, "tensor_copy", bvrep_sb, ps_bv[:])

            with nc.Block() as blk:
                pe0 = n["pe"]
                v0 = n["v"]

                @blk.tensor
                def _(t):
                    for c in range(TC):
                        if c >= 2:
                            t.wait_ge(s_v, v0 + c - 1)
                        ins = None
                        for k in range(KD):
                            for nn in range(2):
                                ins = t.matmul(
                                    ps_v[:, (c % 2) * 1024 + nn * 512:(c % 2) * 1024 + (nn + 1) * 512],
                                    act_bf[:, k * S + c * 128: k * S + (c + 1) * 128],
                                    wv_sb[:, k * 1024 + nn * 512: k * 1024 + (nn + 1) * 512],
                                    start=(k == 0), stop=(k == KD - 1))
                        ins.then_inc(s_pe, 1)

                n["pe"] += TC

                @blk.vector
                def _(v):
                    for c in range(TC):
                        v.wait_ge(s_pe, pe0 + c + 1)
                        vop(v, "tensor_add",
                            vt3[:, c * 16:(c + 1) * 16, 0:64],
                            ps_v[:, (c % 2) * 1024:(c % 2) * 1024 + 1024]
                            .rearrange("p (h e) -> p h e", e=64),
                            bvrep_sb.rearrange("p (h e) -> p h e", e=64))

        with nc.psum_tensor("ps_n16", [16, S], F32) as ps_n16:
            for which, src_sb, col in (("q", q_sb, 0), ("k", k_sb, 1)):
                with nc.Block() as blk:
                    v0 = n["v"]
                    pe0 = n["pe"]

                    @blk.vector
                    def _(v, src_sb=src_sb, pe0=pe0):
                        for j in range(NPAIR):
                            if j >= 1:
                                v.wait_ge(s_pe, pe0 + j)
                            vop(v, "tensor_mul", q2buf,
                                src_sb[:, j * S:(j + 1) * S],
                                src_sb[:, j * S:(j + 1) * S])

                    @blk.tensor
                    def _(t, v0=v0):
                        for j in range(NPAIR):
                            t.wait_ge(s_v, v0 + j + 1)
                            ins = None
                            for nn in range(4):
                                ins = t.matmul(
                                    ps_n16[:, nn * 512:(nn + 1) * 512],
                                    sel16_sb[:, j * 16:(j + 1) * 16],
                                    q2buf[:, nn * 512:(nn + 1) * 512],
                                    start=(j == 0), stop=(j == NPAIR - 1))
                            ins.then_inc(s_pe, 1)

                    n["pe"] += NPAIR

                    @blk.vector
                    def _(v, col=col):
                        v.wait_ge(s_pe, n["pe"])
                        vop(v, "reduce_max", qm2[:, col:col + 1], ps_n16[:], axis=AX.X)

        with (
            nc.psum_tensor("ps_t16", [1, 16], F32) as ps_t16,
            nc.psum_tensor("ps_crep", [128, 8], F32) as ps_crep,
        ):
            with nc.Block() as blk:
                @blk.vector
                def _(v):
                    vop(v, "tensor_add", qm2[:, 0:1], qm2[:, 0:1], qm2[:, 1:2])

                @blk.tensor
                def _(t):
                    t.wait_ge(s_v, n["v"])
                    t.transpose(ps_t16[:], qm2[:, 0:1], id16_sb).then_inc(s_pe, 1)

                n["pe"] += 1

                @blk.vector
                def _(v):
                    v.wait_ge(s_pe, n["pe"])
                    vop(v, "tensor_copy", crow[:, 16:32], ps_t16[:])
                    vop(v, "tensor_max", crow[:, 0:8],
                        crow[:, 16:32:2], crow[:, 17:32:2])
                    vop(v, "tensor_scalar_mul", crow[:, 0:8], crow[:, 0:8], -0.505)

                @blk.tensor
                def _(t):
                    t.wait_ge(s_v, n["v"])
                    t.matmul(ps_crep[:], ones_sb, crow[0:1, 0:8],
                             start=True, stop=True).then_inc(s_pe, 1)

                n["pe"] += 1

                @blk.vector
                def _(v):
                    v.wait_ge(s_pe, n["pe"])
                    vop(v, "tensor_copy", cneg, ps_crep[:])

        TQ = S // 4
        with (
            nc.psum_tensor("ps_s", [128, 2 * 1024], F32) as ps_s,
            nc.psum_tensor("ps_aE", [65, TQ], F32) as ps_aE,
            nc.psum_tensor("ps_aO", [65, TQ], F32) as ps_aO,
        ):
            exp0 = n["exp"]
            att0 = n["att"]
            scx0 = n["scx"]
            dcp0 = n["dcp"]
            NI = NPAIR * 4
            with nc.Block() as blk:
                @blk.tensor
                def _(t):
                    def emit_att(I, c):
                        j = I // 4
                        t.wait_ge(s_exp, exp0 + I * TC + c + 1)
                        if c == 0 and I >= 1:
                            t.wait_ge(s_dcp, dcp0 + I)
                        sb = (c % 4) * 1024
                        ins = None
                        for hh, ps_a in ((0, ps_aE), (1, ps_aO)):
                            vslot = (c * 16 + 2 * j + hh) * 65
                            ins = t.matmul(
                                ps_a[:, :],
                                vt_sb[:, vslot: vslot + 65],
                                pbuf[:, sb + hh * TQ: sb + (hh + 1) * TQ],
                                start=(c == 0), stop=(c == TC - 1))
                        ins.then_inc(s_att, 1)

                    for I in range(NI):
                        j, qq = I // 4, I % 4
                        for c in range(TC):
                            wv_ = exp0 + I * TC + c - 1
                            if wv_ > 0:
                                t.wait_ge(s_exp, wv_)
                            bufo = (c % 2) * 1024
                            ins = None
                            for hh in range(2):
                                p0 = hh * 64
                                ins = t.matmul(
                                    ps_s[:, bufo + hh * TQ: bufo + (hh + 1) * TQ],
                                    k_sb[p0:p0 + 64, j * S + c * 128: j * S + (c + 1) * 128],
                                    q_sb[p0:p0 + 64, j * S + qq * TQ: j * S + (qq + 1) * TQ],
                                    start=True, stop=True)
                            ins.then_inc(s_scx, 1)
                            if c >= 2:
                                emit_att(I, c - 2)
                        emit_att(I, TC - 2)
                        emit_att(I, TC - 1)

                n["scx"] += NI * TC
                n["att"] += NI * TC

                @blk.scalar
                def _(sc):
                    for I in range(NI):
                        j = I // 4
                        for c in range(TC):
                            idx = I * TC + c
                            sc.wait_ge(s_scx, scx0 + idx + 1)
                            wv_ = att0 + I * TC + c - 3
                            if wv_ > 0:
                                sc.wait_ge(s_att, wv_)
                            sc.activation(pbuf[:, (c % 4) * 1024:(c % 4) * 1024 + 1024],
                                          ps_s[:, (c % 2) * 1024:(c % 2) * 1024 + 1024],
                                          AF.Exp,
                                          bias=cneg[:, j:j + 1]).then_inc(s_exp, 1)
                    n["exp"] += NI * TC

                @blk.vector
                def _(v):
                    for I in range(NI):
                        j, qq = I // 4, I % 4
                        v.wait_ge(s_att, att0 + (I + 1) * TC)
                        for kk, ps_a in ((2 * I, ps_aE), (2 * I + 1, ps_aO)):
                            if kk >= 2:
                                v.wait_ge(sdd[kk % 2], 16 * (kk // 2))
                            vop(v, "tensor_copy",
                                dtmp[64:65, (kk % 2) * 1024:(kk % 2) * 1024 + TQ],
                                ps_a[64:65, :])
                        vop(v, "tensor_copy",
                            act_bf[0:64, j * S + qq * TQ: j * S + (qq + 1) * TQ],
                            ps_aE[0:64, :])
                        v.wait_ge(s_v, n["v"])
                        v.tensor_copy(
                            act_bf[64:128, j * S + qq * TQ: j * S + (qq + 1) * TQ],
                            ps_aO[0:64, :]).then_inc(s_dcp, 1)
                        n["dcp"] += 1

                v3d = n["v"]

                @blk.sync
                def _(sy):
                    sy.dma_start(selpair_sb, selpair_c).then_inc(s_const, 16)
                    vbase = v3d - NI * 3
                    for I in range(NI):
                        j, qq = I // 4, I % 4
                        for kk, row in ((2 * I, 2 * j), (2 * I + 1, 2 * j + 1)):
                            sy.wait_ge(s_v, vbase + I * 3 + (kk % 2) + 1)
                            if kk >= 2:
                                sy.wait_ge(sdd[kk % 2], 16 * (kk // 2))
                            sy.dma_start(
                                denom_sb[row:row + 1, qq * TQ:(qq + 1) * TQ],
                                dtmp[64:65, (kk % 2) * 1024:(kk % 2) * 1024 + TQ]
                            ).then_inc(sdd[kk % 2], 16)

        with nc.psum_tensor("ps_rep", [128, 2 * S], F32) as ps_rep:
            with nc.Block() as blk:
                @blk.vector
                def _(v):
                    v.wait_ge(s_dd0, 16 * 32)
                    v.wait_ge(s_dd1, 16 * 32)
                    vop(v, "reciprocal", rtile, denom_sb)

                v0 = n["v"]
                pe0 = n["pe"]

                @blk.tensor
                def _(t):
                    t.wait_ge(s_v, v0)
                    t.wait_ge(s_const, 64)
                    for j in range(NPAIR):
                        if j >= 2:
                            t.wait_ge(s_v, v0 + j - 1)
                        ins = None
                        for nn in range(4):
                            ins = t.matmul(
                                ps_rep[:, (j % 2) * S + nn * 512:(j % 2) * S + (nn + 1) * 512],
                                selpair_sb[:, j * 128:(j + 1) * 128],
                                rtile[:, nn * 512:(nn + 1) * 512],
                                start=True, stop=True)
                        ins.then_inc(s_pe, 1)

                n["pe"] += NPAIR

                @blk.vector
                def _(v):
                    for j in range(NPAIR):
                        v.wait_ge(s_pe, pe0 + j + 1)
                        vop(v, "tensor_mul", attn_n[:, j * S:(j + 1) * S],
                            act_bf[:, j * S:(j + 1) * S],
                            ps_rep[:, (j % 2) * S:(j % 2) * S + S])

        with nc.psum_tensor("ps_sxb", [1, S], F32) as ps_sxb:
            with nc.psum_tensor("ps_wo", [128, 2 * 1024], F32) as ps_wo:
                with nc.Block() as blk:
                    v0 = n["v"]
                    pe0 = n["pe"]
                    sq0 = n["sc"]

                    @blk.gpsimd
                    def _(g):
                        g.wait_ge(s_dd0, 16 * 32)
                        g.wait_ge(s_dd1, 16 * 32)
                        g.dma_start(wb0[:, 0: KD * 128], w1_d[0]).then_inc(s_wl0, 16)
                        g.dma_start(wb1[:, 0: KD * 128], w1_d[1]).then_inc(s_wl1, 16)

                    @blk.sync
                    def _(sy):
                        for m in range(KD):
                            if m >= 2:
                                sy.wait_ge(s_v, v0 + 2 * (m - 1))
                            sy.dma_start(xrb[:, (m % 2) * S:(m % 2) * S + S],
                                         xst[:, m * S:(m + 1) * S]).then_inc(sxr[m % 2], 16)

                    @blk.tensor
                    def _(t):
                        def emit_stats_sx(k):
                            t.wait_ge(s_v, v0 + 2 * k + 2)
                            ins = None
                            for nn in range(4):
                                ins = t.matmul(ps_sxb[:, nn * 512:(nn + 1) * 512],
                                               onescol_sb,
                                               x2buf[:, k * S + nn * 512: k * S + (nn + 1) * 512],
                                               start=(k == 0), stop=(k == KD - 1))
                            ins.then_inc(s_pe, 1)

                        for m in range(KD):
                            for hf in range(2):
                                g2 = 2 * m + hf
                                if g2 >= 2:
                                    t.wait_ge(s_v, v0 + g2 - 1)
                                ins = None
                                for j in range(NPAIR):
                                    for nn in range(2):
                                        ins = t.matmul(
                                            ps_wo[:, (g2 % 2) * 1024 + nn * 512:(g2 % 2) * 1024 + (nn + 1) * 512],
                                            wo_sb[:, m * 1024 + j * 128: m * 1024 + (j + 1) * 128],
                                            attn_n[:, j * S + hf * 1024 + nn * 512: j * S + hf * 1024 + (nn + 1) * 512],
                                            start=(j == 0), stop=(j == NPAIR - 1))
                                ins.then_inc(s_pe, 1)
                            if m >= 1:
                                emit_stats_sx(m - 1)
                        emit_stats_sx(KD - 1)

                    n["pe"] += 2 * KD + KD

                    @blk.vector
                    def _(v):
                        for m in range(KD):
                            for hf in range(2):
                                g2 = 2 * m + hf
                                v.wait_ge(s_pe, pe0 + g2 + 1 + max(m - 1, 0))
                                v.wait_ge(sxr[m % 2], 16 * (m // 2 + 1))
                                vop(v, "scalar_tensor_tensor",
                                    x2buf[:, m * S + hf * 1024: m * S + hf * 1024 + 1024],
                                    ps_wo[:, (g2 % 2) * 1024:(g2 % 2) * 1024 + 1024],
                                    bo_sb[:, m:m + 1],
                                    xrb[:, (m % 2) * S + hf * 1024:(m % 2) * S + (hf + 1) * 1024],
                                    ALU.add, ALU.add)

                    @blk.scalar
                    def _(sc):
                        for k in range(2):
                            sc.wait_ge(s_v, v0 + 2 * k + 2)
                            scop(sc, "activation", sqbf2[:, (k % 2) * S:(k % 2) * S + S],
                                 x2buf[:, k * S:(k + 1) * S], AF.Square)

                    @blk.sync
                    def _(sy):
                        for m in range(KD):
                            sy.wait_ge(s_v, v0 + 2 * m + 2)
                            sy.dma_start(x2hbm[:, m * S:(m + 1) * S],
                                         x2buf[:, m * S:(m + 1) * S]).then_inc(s_x2st, 16)

            with nc.psum_tensor("ps_sqb", [1, S], F32) as ps_sqb:
                with nc.Block() as blk:
                    sq1 = n["sc"] - 2
                    pe1 = n["pe"]

                    @blk.scalar
                    def _(sc):
                        for k in range(2, KD):
                            sc.wait_ge(s_pe, pe1 + k - 1)
                            scop(sc, "activation", sqbf2[:, (k % 2) * S:(k % 2) * S + S],
                                 x2buf[:, k * S:(k + 1) * S], AF.Square)

                    @blk.tensor
                    def _(t):
                        for k in range(KD):
                            t.wait_ge(s_sc, sq1 + k + 1)
                            ins = None
                            for nn in range(4):
                                ins = t.matmul(ps_sqb[:, nn * 512:(nn + 1) * 512],
                                               onescol_bf,
                                               sqbf2[:, (k % 2) * S + nn * 512: (k % 2) * S + (nn + 1) * 512],
                                               start=(k == 0), stop=(k == KD - 1))
                            ins.then_inc(s_pe, 1)

                    n["pe"] += KD

                    @blk.vector
                    def _(v):
                        v.wait_ge(s_pe, n["pe"])
                        rowm = rowbuf2[0:1, 0:S]
                        rowr = rowbuf2[0:1, S:2 * S]
                        rowu = xsq2[0:1, S:2 * S]
                        vop(v, "tensor_scalar_mul", rowm, ps_sxb[:], 1.0 / D)
                        vop(v, "tensor_mul", rowu, rowm, rowm)
                        vop(v, "scalar_tensor_tensor", rowr, ps_sqb[:], 1.0 / D,
                            rowu, ALU.mult, ALU.subtract)
                        vop(v, "tensor_scalar_add", rowu, rowr, LN_EPS)

                    @blk.scalar
                    def _(sc):
                        rowt = xsq2[0:1, 0:S]
                        rowu = xsq2[0:1, S:2 * S]
                        sc.wait_ge(s_v, n["v"])
                        scop(sc, "activation", rowt, rowu, AF.Ln)
                        scop(sc, "activation", rowt, rowt, AF.Exp, scale=-0.5)

                    @blk.vector
                    def _(v):
                        rowt = xsq2[0:1, 0:S]
                        rowu = xsq2[0:1, S:2 * S]
                        rowr = rowbuf2[0:1, S:2 * S]
                        v.wait_ge(s_sc, n["sc"])
                        vop(v, "tensor_mul", rowr, rowt, rowt)
                        vop(v, "tensor_mul", rowr, rowr, rowu)
                        vop(v, "tensor_scalar", rowr, rowr, -0.5, 1.5, ALU.mult, ALU.add)
                        vop(v, "tensor_mul", rowr, rowt, rowr)

        with (
            nc.psum_tensor("ps_mrepb", [128, S], F32) as ps_mrep,
            nc.psum_tensor("ps_rrepb", [128, S], F32) as ps_rrep,
        ):
            with nc.Block() as blk:
                @blk.tensor
                def _(t):
                    rowm = rowbuf2[0:1, 0:S]
                    rowr = rowbuf2[0:1, S:2 * S]
                    ins = None
                    for nn in range(4):
                        t.matmul(ps_mrep[:, nn * 512:(nn + 1) * 512], ones_sb,
                                 rowm[0:1, nn * 512:(nn + 1) * 512],
                                 start=True, stop=True)
                    for nn in range(4):
                        ins = t.matmul(ps_rrep[:, nn * 512:(nn + 1) * 512], ones_sb,
                                       rowr[0:1, nn * 512:(nn + 1) * 512],
                                       start=True, stop=True)
                    ins.then_inc(s_pe, 1)

                n["pe"] += 1

                @blk.vector
                def _(v):
                    v.wait_ge(s_pe, n["pe"])
                    for k in range(KD):
                        vop(v, "tensor_sub", xsq2[:, (k % 2) * S:(k % 2) * S + S],
                            x2buf[:, k * S:(k + 1) * S], ps_mrep[:])
                        vop(v, "tensor_mul", act_bf[:, k * S:(k + 1) * S],
                            xsq2[:, (k % 2) * S:(k % 2) * S + S], ps_rrep[:])

        wb = [wb0, wb1]
        sched = []
        for hf in range(2):
            sched += [("w1", hf, m) for m in range(KH)]
            sched += [("w2", hf, m) for m in range(KH)]
            sched += [("h1", hf, m) for m in range(KH)]
            sched += [("h2", hf, m) for m in range(KH)]
            sched += [("w3", hf, m) for m in range(KD)]
        NS = len(sched)
        wsrc = {"w1": (w1_d, KD), "w2": (w2_d, KD), "h1": (h1_d, KH),
                "h2": (h2_d, KH), "w3": (w3_d, KH)}
        eng = {"w1": "s", "w2": "v", "h1": "s", "h2": "s", "w3": "v"}
        evs_b = [0] * (NS + 1)
        evv_b = [0] * (NS + 1)
        for i, (lay, hf, m) in enumerate(sched):
            evs_b[i + 1] = evs_b[i] + (1 if eng[lay] == "s" else 0)
            evv_b[i + 1] = evv_b[i] + (1 if eng[lay] == "v" else 0)

        mm0 = n["mm"]
        wc0 = n["wc"]
        evs0 = n["evs"]
        evv0 = n["evv"]
        w3_evv_val = [evv0 + evv_b[i] + 1
                      for i, (lay, hf, m) in enumerate(sched) if lay == "w3"]

        with nc.psum_tensor("ps_f", [128, 2 * 1024], F32) as ps_f:
            with nc.Block() as blk:
                @blk.gpsimd
                def _(g):
                    for i, (lay, hf, m) in enumerate(sched):
                        if i < 2:
                            continue
                        src, kc = wsrc[lay]
                        g.wait_ge(s_mm, mm0 + i - 1)
                        g.dma_start(wb[i % 2][:, 0: kc * 128], src[m]).then_inc(swl[i % 2], 16)

                @blk.tensor
                def _(t):
                    for i, (lay, hf, m) in enumerate(sched):
                        src, kc = wsrc[lay]
                        t.wait_ge(swl[i % 2], 16 * (i // 2 + 1))
                        if m == 0:
                            if evs_b[i] > 0:
                                t.wait_ge(s_evs, evs0 + evs_b[i])
                            if evv_b[i] > 0:
                                t.wait_ge(s_evv, evv0 + evv_b[i])
                        elif i >= 2:
                            lay2 = sched[i - 2][0]
                            if eng[lay2] == "s":
                                t.wait_ge(s_evs, evs0 + evs_b[i - 2] + 1)
                            else:
                                t.wait_ge(s_evv, evv0 + evv_b[i - 2] + 1)
                        if lay in ("w1", "w2"):
                            rhs, rb, rstride = act_bf, hf * THALF, S
                        elif lay == "h1":
                            rhs, rb, rstride = hbufB, 0, THALF
                        elif lay == "h2":
                            rhs, rb, rstride = hbufA, 0, THALF
                        else:
                            rhs, rb, rstride = hbufB, 0, THALF
                        ins = None
                        for k in range(kc):
                            for nn in range(2):
                                ins = t.matmul(
                                    ps_f[:, (i % 2) * 1024 + nn * 512:(i % 2) * 1024 + (nn + 1) * 512],
                                    wb[i % 2][:, k * 128:(k + 1) * 128],
                                    rhs[:, k * rstride + rb + nn * 512: k * rstride + rb + (nn + 1) * 512],
                                    start=(k == 0), stop=(k == kc - 1))
                        ins.then_inc(s_mm, 1)

                n["wc"] += NS
                n["mm"] += NS

                @blk.scalar
                def _(sc):
                    sc.wait_ge(s_x2st, 16 * KD)
                    for i, (lay, hf, m) in enumerate(sched):
                        if eng[lay] != "s":
                            continue
                        sc.wait_ge(s_mm, mm0 + i + 1)
                        dst = hbufA if lay in ("w1", "h1") else hbufB
                        func = AF.Silu if lay == "w1" else AF.Gelu
                        ssb = {"w1": s1_sb, "h1": sh1_sb, "h2": sh2_sb}[lay]
                        sc.activation(dst[:, m * THALF:(m + 1) * THALF],
                                      ps_f[:, (i % 2) * 1024:(i % 2) * 1024 + 1024],
                                      func, scale=ssb[:, m:m + 1]).then_inc(s_evs, 1)
                        n["evs"] += 1

                @blk.vector
                def _(v):
                    v.wait_ge(s_x2st, 16 * KD)
                    nxr = 0
                    nst = 0
                    for i, (lay, hf, m) in enumerate(sched):
                        if eng[lay] != "v":
                            continue
                        v.wait_ge(s_mm, mm0 + i + 1)
                        v.wait_ge(s_v, n["v"])
                        if lay == "w2":
                            v.scalar_tensor_tensor(
                                hbufB[:, m * THALF:(m + 1) * THALF],
                                ps_f[:, (i % 2) * 1024:(i % 2) * 1024 + 1024],
                                s2_sb[:, m:m + 1],
                                hbufA[:, m * THALF:(m + 1) * THALF],
                                ALU.mult, ALU.mult).then_inc(s_evv, 1)
                            n["evv"] += 1
                        else:
                            v.wait_ge(sxr[nxr % 2], 16 * (KD // 2) + 16 * (nxr // 2 + 1))
                            if nst >= 2:
                                v.wait_ge(sout[nst % 2], 16 * (nst // 2))
                            v.scalar_tensor_tensor(
                                ostg[:, (nst % 2) * 1024:(nst % 2) * 1024 + 1024],
                                ps_f[:, (i % 2) * 1024:(i % 2) * 1024 + 1024],
                                s3_sb[:, m:m + 1],
                                x2rb[:, (nxr % 2) * 1024:(nxr % 2) * 1024 + 1024],
                                ALU.mult, ALU.add).then_inc(s_evv, 1)
                            n["evv"] += 1
                            nxr += 1
                            nst += 1

                @blk.sync
                def _(sy):
                    nxr = 0
                    nst = 0
                    for i, (lay, hf, m) in enumerate(sched):
                        if lay != "w3":
                            continue
                        if nxr == 0:
                            sy.wait_ge(s_x2st, 16 * KD)
                        if nxr >= 2:
                            sy.wait_ge(s_evv, w3_evv_val[nxr - 2])
                        sy.dma_start(x2rb[:, (nxr % 2) * 1024:(nxr % 2) * 1024 + 1024],
                                     x2hbm[:, m * S + hf * 1024: m * S + hf * 1024 + 1024]
                                     ).then_inc(sxr[nxr % 2], 16)
                        nxr += 1
                        sy.wait_ge(s_evv, w3_evv_val[nst])
                        sy.dma_start(out_fm[:, m * S + hf * 1024: m * S + hf * 1024 + 1024],
                                     ostg[:, (nst % 2) * 1024:(nst % 2) * 1024 + 1024]
                                     ).then_inc(sout[nst % 2], 16)
                        nst += 1
                    sy.wait_ge(sout[0], 16 * 8)
                    sy.wait_ge(sout[1], 16 * 8)

    return nc


def kernel(**inputs):
    ins = inputs
    x = np.asarray(ins["L0_attn_x"], np.float32)

    def stage_w(wmat):
        WT = np.ascontiguousarray(np.asarray(wmat, np.float32).T)
        kin, kout = WT.shape
        KC, MT = kin // 128, kout // 128
        return np.ascontiguousarray(
            WT.reshape(KC, 128, MT, 128).transpose(2, 1, 0, 3).reshape(MT, 128, KC * 128))

    wq_m = np.asarray(ins["L0_attn_wq"], np.float32).transpose(1, 0, 2).reshape(D, D)
    wk_m = np.asarray(ins["L0_attn_wk"], np.float32).transpose(1, 0, 2).reshape(D, D)
    wv_m = np.asarray(ins["L0_attn_wv"], np.float32).transpose(1, 0, 2).reshape(D, D)
    wq_st = np.ascontiguousarray(
        wq_m.reshape(KD, 128, NPAIR, 128).transpose(1, 2, 0, 3).reshape(128, 8192))
    wk_st = np.ascontiguousarray(
        wk_m.reshape(KD, 128, NPAIR, 128).transpose(1, 2, 0, 3).reshape(128, 8192))
    wv_st = np.ascontiguousarray(
        wv_m.reshape(KD, 128, D).transpose(1, 0, 2).reshape(128, 8192))
    woT = np.ascontiguousarray(np.asarray(ins["L0_attn_wo"], np.float32).T)
    wo_st = np.ascontiguousarray(
        woT.reshape(KD, 128, KD, 128).transpose(1, 2, 0, 3).reshape(128, 8192))

    bq_st = np.ascontiguousarray(
        np.asarray(ins["L0_attn_bq"], np.float32).reshape(NPAIR, 2, 64)
        .transpose(1, 2, 0).reshape(128, NPAIR))
    bk_st = np.ascontiguousarray(
        np.asarray(ins["L0_attn_bk"], np.float32).reshape(NPAIR, 2, 64)
        .transpose(1, 2, 0).reshape(128, NPAIR))
    bv_st = np.ascontiguousarray(np.asarray(ins["L0_attn_bv"], np.float32).reshape(1, D))
    bo_st = np.ascontiguousarray(np.asarray(ins["L0_attn_bo"], np.float32).reshape(KD, 128).T)

    w1_st = stage_w(ins["L0_ffn_w1_w"])
    w2_st = stage_w(ins["L0_ffn_w2_w"])
    h1_st = stage_w(ins["L0_ffn_h1_w"])
    h2_st = stage_w(ins["L0_ffn_h2_w"])
    w3_st = stage_w(ins["L0_ffn_w3_w"])
    s1_st = np.ascontiguousarray(np.asarray(ins["L0_ffn_w1_s"], np.float32).reshape(KH, 128).T)
    s2_st = np.ascontiguousarray(np.asarray(ins["L0_ffn_w2_s"], np.float32).reshape(KH, 128).T)
    sh1_st = np.ascontiguousarray(np.asarray(ins["L0_ffn_h1_s"], np.float32).reshape(KH, 128).T)
    sh2_st = np.ascontiguousarray(np.asarray(ins["L0_ffn_h2_s"], np.float32).reshape(KH, 128).T)
    s3_st = np.ascontiguousarray(np.asarray(ins["L0_ffn_w3_s"], np.float32).reshape(KD, 128).T)

    nc = build_nc()
    in_maps = []
    for c in range(B):
        xc = np.ascontiguousarray(
            x[c].T.reshape(KD, 128, S).transpose(1, 0, 2).reshape(128, KD * S))
        in_maps.append({
            "xst": xc, "wq": wq_st, "wk": wk_st, "wv": wv_st, "wo": wo_st,
            "bq": bq_st, "bk": bk_st, "bvr": bv_st, "bo": bo_st,
            "w1": w1_st, "w2": w2_st, "h1": h1_st, "h2": h2_st, "w3": w3_st,
            "s1": s1_st, "s2": s2_st, "sh1": sh1_st, "sh2": sh2_st, "s3": s3_st,
        })
    res = run_bass_kernel_spmd(nc, in_maps, core_ids=list(range(B)))
    outs = []
    for c in range(B):
        o = np.asarray(res.results[c]["out_fm"], np.float32)
        outs.append(o.reshape(128, KD, S).transpose(1, 0, 2).reshape(D, S).T)
    return np.ascontiguousarray(np.stack(outs, 0))
